# version 02
# baseline (speedup 1.0000x reference)
import contextlib

import numpy as np
import ml_dtypes

import concourse.bass as bass
import concourse.mybir as mybir
from concourse.alu_op_type import AluOpType
from concourse.bass_utils import run_bass_kernel_spmd

F32 = mybir.dt.float32
BF16 = mybir.dt.bfloat16
I32 = mybir.dt.int32
AF = mybir.ActivationFunctionType
AX = mybir.AxisListType
OP = AluOpType

D = 4096
I_DIM = 11008
IT = 86
NG = 32
NCORES = 8
TOK = 1024
T_SUP = 512
N_SUP = 2
NTT = 4
OCN = 8
SC = float(1.0 / np.sqrt(128.0))
EPS = 1e-5
QUARTERS = [(0, 22), (22, 22), (44, 21), (65, 21)]
QUADS = [(q * 4, 4) for q in range(21)] + [(84, 2)]

ENGS = ("sync", "tensor", "vector", "scalar", "gpsimd")


class Tape:
    def __init__(self):
        self.ops = {e: [] for e in ENGS}
        self.cnt = {}
        self.enabled = True

    def emit(self, eng, method, *args, waits=(), inc=None, **kwargs):
        if not self.enabled:
            return
        if inc is not None:
            s, amt = inc
            self.cnt[s] = self.cnt.get(s, 0) + amt
        self.ops[eng].append((tuple(waits), method, args, kwargs, inc))

    def val(self, sem):
        return self.cnt.get(sem, 0)


def build_nc(do_stage0=True, do_prep=True, n_sup=N_SUP, passes=(0, 1, 2),
             dbg_scratch=None):
    nc = bass.Bass("TRN2", target_bir_lowering=False,
                   detect_race_conditions=False)

    def inp(name, shape, dt):
        return nc.dram_tensor(name, list(shape), dt, kind="ExternalInput").ap()

    x_in = inp("x", (TOK, D), F32)
    qwg = inp("qwg", (I_DIM, D), I32)
    qwu = inp("qwu", (I_DIM, D), I32)
    qwd = inp("qwd", (D, I_DIM), I32)
    sg_in = inp("sg", (I_DIM, 32), F32)
    zg_in = inp("zg", (I_DIM, 32), F32)
    su_in = inp("su", (I_DIM, 32), F32)
    zu_in = inp("zu", (I_DIM, 32), F32)
    sd_in = inp("sd", (D, IT), F32)
    zd_in = inp("zd", (D, IT), F32)
    wq_in = inp("wq", (D, 128 * 32), F32)
    bq_in = inp("bq", (128, 32), F32)
    wk_in = inp("wk", (D, 128), F32)
    bk_in = inp("bk", (128, 1), F32)
    wv_in = inp("wv", (D, 128), F32)
    bv_in = inp("bv", (128, 1), F32)
    eye_b = inp("eye_b", (128, 128), BF16)
    eye_f = inp("eye_f", (128, 128), F32)
    out = nc.dram_tensor("out", [TOK, D], F32, kind="ExternalOutput").ap()

    wgT_s = nc.dram_tensor("wgT_s", [IT, 128, D], BF16, kind="Internal").ap()
    wuT_s = nc.dram_tensor("wuT_s", [IT, 128, D], BF16, kind="Internal").ap()
    wdT_s = nc.dram_tensor("wdT_s", [OCN, IT, 128, 512], BF16, kind="Internal").ap()

    tape = Tape()
    with contextlib.ExitStack() as top:
        E = top.enter_context
        s_pe = E(nc.semaphore("s_pe"))
        s_vec = E(nc.semaphore("s_vec"))
        s_sca = E(nc.semaphore("s_sca"))
        s_eye = E(nc.semaphore("s_eye"))
        s0_q = [E(nc.semaphore(f"s0_q{i}")) for i in range(2)]
        s0_d = [E(nc.semaphore(f"s0_d{i}")) for i in range(2)]
        s0_z = E(nc.semaphore("s0_z"))
        s0_w = [E(nc.semaphore(f"s0_w{i}")) for i in range(2)]
        s0_y = [E(nc.semaphore(f"s0_y{i}")) for i in range(2)]
        p_w = [E(nc.semaphore(f"p_w{i}")) for i in range(2)]
        p_b = E(nc.semaphore("p_b"))
        s_x = E(nc.semaphore("s_x"))
        m_g = [E(nc.semaphore(f"m_g{i}")) for i in range(2)]
        m_u = [E(nc.semaphore(f"m_u{i}")) for i in range(2)]
        m_b = [E(nc.semaphore(f"m_b{i}")) for i in range(2)]
        s_st = E(nc.semaphore("s_st"))

        eye_sb = E(nc.sbuf_tensor("eye_sb", [128, 128], BF16))
        eye_fb = E(nc.sbuf_tensor("eye_fb", [128, 128], F32))
        tape.emit("sync", "dma_start", out=eye_sb[:, :], in_=eye_b, inc=(s_eye, 16))
        tape.emit("sync", "dma_start", out=eye_fb[:, :], in_=eye_f, inc=(s_eye, 16))

        tape.enabled = do_stage0
        with contextlib.ExitStack() as s0:
            E0 = s0.enter_context
            qb = [E0(nc.sbuf_tensor(f"qb{i}", [128, D], I32)) for i in range(2)]
            zb = [E0(nc.sbuf_tensor(f"zb{i}", [128, 32], F32)) for i in range(2)]
            sbu = [E0(nc.sbuf_tensor(f"sbu{i}", [128, 32], F32)) for i in range(2)]
            dqb = [E0(nc.sbuf_tensor(f"dqb{i}", [128, D], BF16)) for i in range(2)]
            asm = [E0(nc.sbuf_tensor(f"asm{i}", [128, D], BF16)) for i in range(2)]
            ptS = [E0(nc.psum_tensor(f"ptS{i}", [128, 1024], BF16)) for i in range(2)]

            bank_free = {0: None, 1: None}
            bank_id = [0]

            def do_pack(src_tiles, dst_ap, first_waits):
                bk = bank_id[0] % 2
                bank_id[0] += 1
                w = list(first_waits)
                if bank_free[bk] is not None:
                    w.append((s_sca, bank_free[bk]))
                n = len(src_tiles)
                for j, sap in enumerate(src_tiles):
                    tape.emit("tensor", "transpose",
                              ptS[bk][:, j * 128:(j + 1) * 128], sap,
                              eye_sb[:, :], waits=w if j == 0 else (),
                              inc=(s_pe, 1) if j == n - 1 else None)
                pe_v = tape.val(s_pe)
                if dst_ap.ndim == 4:
                    qn = dst_ap.shape[2]
                    src = ptS[bk][:, 0:n * 128].rearrange(
                        "p (a b c) -> p a b c", b=qn, c=128)
                else:
                    src = ptS[bk][:, 0:n * 128].rearrange(
                        "p (a c) -> p a c", c=128)
                tape.emit("scalar", "copy", dst_ap, src,
                          waits=[(s_pe, pe_v)], inc=(s_sca, 1))
                bank_free[bk] = tape.val(s_sca)
                return pe_v

            dq_done = {}
            tr_done = {}
            out_done = {}
            for mi, (qw, z_i, s_i, scr) in enumerate(
                    [(qwg, zg_in, sg_in, wgT_s), (qwu, zu_in, su_in, wuT_s)]):
                for it in range(IT):
                    b = it % 2
                    key = (mi, it)
                    pkey = (mi, it - 2) if it >= 2 else (mi - 1, IT - 2 + it)
                    w = []
                    if pkey in dq_done:
                        w.append((s_vec, dq_done[pkey]))
                    tape.emit("sync", "dma_start", out=qb[b][:, :],
                              in_=qw[it * 128:(it + 1) * 128, :],
                              waits=w, inc=(s0_q[b], 16))
                    tape.emit("sync", "dma_start", out=zb[b][:, :],
                              in_=z_i[it * 128:(it + 1) * 128, :],
                              inc=(s0_q[b], 16))
                    tape.emit("sync", "dma_start", out=sbu[b][:, :],
                              in_=s_i[it * 128:(it + 1) * 128, :],
                              inc=(s0_q[b], 16))
                    ld = tape.val(s0_q[b])
                    w = [(s0_q[b], ld)]
                    if pkey in tr_done:
                        w.append((s_pe, tr_done[pkey]))
                    for g in range(NG):
                        tape.emit("vector", "tensor_scalar",
                                  dqb[b][:, g * 128:(g + 1) * 128],
                                  qb[b][:, g * 128:(g + 1) * 128],
                                  zb[b][:, g:g + 1], sbu[b][:, g:g + 1],
                                  OP.subtract, OP.mult,
                                  waits=w if g == 0 else (),
                                  inc=(s_vec, 1) if g == NG - 1 else None)
                    dq_done[key] = tape.val(s_vec)
                    last_pe = None
                    for pkk in range(4):
                        fw = [(s_vec, dq_done[key])] if pkk == 0 else []
                        if pkk == 0 and it == 0 and mi == 0:
                            fw.append((s_eye, 32))
                        if pkk == 0 and pkey in out_done:
                            fw.append((s0_w[b], out_done[pkey]))
                        tiles = [dqb[b][:, g * 128:(g + 1) * 128]
                                 for g in range(pkk * 8, pkk * 8 + 8)]
                        dst = asm[b][:, pkk * 1024:(pkk + 1) * 1024]\
                            .rearrange("p (a c) -> p a c", c=128)
                        last_pe = do_pack(tiles, dst, fw)
                    tr_done[key] = last_pe
                    tape.emit("scalar", "dma_start", out=scr[it],
                              in_=asm[b][:, :],
                              waits=[(s_sca, bank_free[(bank_id[0] - 1) % 2])],
                              inc=(s0_w[b], 16))
                    out_done[key] = tape.val(s0_w[b])

            sdh = E0(nc.sbuf_tensor("sdh", [128, 32 * IT], F32))
            zdx = E0(nc.sbuf_tensor("zdx", [128, 32 * IT], F32))
            sd_r = sd_in.rearrange("(ot p) g -> p ot g", p=128)
            zd_r = zd_in.rearrange("(ot p) g -> p ot g", p=128)
            sdh3 = sdh[:, :].rearrange("p (ot g) -> p ot g", g=IT)
            zdx3 = zdx[:, :].rearrange("p (ot g) -> p ot g", g=IT)
            tape.emit("gpsimd", "dma_start", out=zdx3, in_=zd_r,
                      inc=(s0_z, 16))
            tape.emit("gpsimd", "dma_start", out=sdh3, in_=sd_r,
                      inc=(s0_z, 16))
            tape.emit("vector", "tensor_scalar", sdh[:, :], sdh[:, :], 0.5,
                      None, OP.mult, waits=[(s0_z, 32)], inc=(s_vec, 1))
            sdh_done = tape.val(s_vec)

            qd = [E0(nc.sbuf_tensor(f"qd{i}", [128, 512], I32)) for i in range(2)]
            dqd = [E0(nc.sbuf_tensor(f"dqd{i}", [128, 512], BF16)) for i in range(2)]
            asm4 = [E0(nc.sbuf_tensor(f"asm4_{i}", [128, 4 * D], BF16))
                    for i in range(2)]

            dq_dn = {}
            tr_dn = {}
            out_dn = {}
            for qi, (it0, qlen) in enumerate(QUADS):
                ab = qi % 2
                nper = 2
                pend = []
                first_evac = True
                for ot in range(32):
                    slot = ot % 2
                    kprev = (qi, ot - 2) if ot >= 2 else (qi - 1, 30 + ot)
                    w = []
                    if kprev in dq_dn:
                        w.append((s_vec, dq_dn[kprev]))
                    tape.emit("sync", "dma_start",
                              out=qd[slot][:, 0:qlen * 128],
                              in_=qwd[ot * 128:(ot + 1) * 128,
                                      it0 * 128:(it0 + qlen) * 128],
                              waits=w, inc=(s0_d[slot], 16))
                    ldv = tape.val(s0_d[slot])
                    w = [(s0_d[slot], ldv)]
                    if qi == 0 and ot == 0:
                        w.append((s_vec, sdh_done))
                    if kprev in tr_dn:
                        w.append((s_pe, tr_dn[kprev]))
                    for itl in range(qlen):
                        g = it0 + itl
                        tape.emit("vector", "tensor_scalar",
                                  dqd[slot][:, itl * 128:(itl + 1) * 128],
                                  qd[slot][:, itl * 128:(itl + 1) * 128],
                                  zdx[:, ot * IT + g:ot * IT + g + 1],
                                  sdh[:, ot * IT + g:ot * IT + g + 1],
                                  OP.subtract, OP.mult,
                                  waits=w if itl == 0 else (),
                                  inc=(s_vec, 1) if itl == qlen - 1 else None)
                    dq_dn[(qi, ot)] = tape.val(s_vec)
                    pend.append((ot, slot, tape.val(s_vec)))
                    if len(pend) == nper:
                        tiles = []
                        fw = [(s_vec, dq_dn[(qi, ot)])]
                        if first_evac and (qi - 2) in out_dn:
                            fw.append((s0_y[ab], out_dn[qi - 2]))
                        for (ot_, sl_, _) in pend:
                            for itl in range(qlen):
                                tiles.append(
                                    dqd[sl_][:, itl * 128:(itl + 1) * 128])
                        ot0 = pend[0][0]
                        dst = asm4[ab][:, :].rearrange(
                            "p (i o j) -> p o i j", o=32, j=128)[
                            :, ot0:ot0 + nper, 0:qlen, :]
                        pe_v = do_pack(tiles, dst, fw)
                        for (ot_, _, _) in pend:
                            tr_dn[(qi, ot_)] = pe_v
                        first_evac = False
                        pend = []
                w = [(s_sca, bank_free[(bank_id[0] - 1) % 2])]
                for itl in range(qlen):
                    for oc in range(OCN):
                        tape.emit("scalar", "dma_start",
                                  out=wdT_s[oc, it0 + itl],
                                  in_=asm4[ab][:, itl * D + oc * 512:
                                               itl * D + (oc + 1) * 512],
                                  waits=w if (itl == 0 and oc == 0) else (),
                                  inc=(s0_y[ab], 16))
                out_dn[qi] = tape.val(s0_y[ab])

        tape.enabled = True
        s0_fence = [(s0_w[0], tape.val(s0_w[0])), (s0_w[1], tape.val(s0_w[1])),
                    (s0_y[0], tape.val(s0_y[0])), (s0_y[1], tape.val(s0_y[1]))]

        if dbg_scratch is not None:
            dbg_out = nc.dram_tensor("dbg_out", [128, 3 * D], BF16,
                                     kind="ExternalOutput").ap()
            dbg_stk = contextlib.ExitStack()
            dbg_sb = dbg_stk.enter_context(
                nc.sbuf_tensor("dbg_sb", [128, 3 * D], BF16))
            it_d, oc_d = dbg_scratch
            tape.emit("sync", "dma_start", out=dbg_sb[:, 0:D],
                      in_=wgT_s[it_d], waits=list(s0_fence),
                      inc=(p_b, 16))
            tape.emit("sync", "dma_start", out=dbg_sb[:, D:2 * D],
                      in_=wuT_s[it_d], inc=(p_b, 16))
            for j in range(OCN):
                tape.emit("sync", "dma_start",
                          out=dbg_sb[:, 2 * D + j * 512:2 * D + (j + 1) * 512],
                          in_=wdT_s[j, oc_d], inc=(p_b, 16))
            dl = tape.val(p_b)
            tape.emit("gpsimd", "dma_start", out=dbg_out, in_=dbg_sb[:, :],
                      waits=[(p_b, dl)], inc=(s_st, 16))
            dbg_stk.close()
            dbg_gps = tape.val(s_st)

        x_sb = E(nc.sbuf_tensor("x_sb", [128, NTT * D], F32))
        xnT = E(nc.sbuf_tensor("xnT", [128, NG * T_SUP], BF16))
        xn_sc = E(nc.sbuf_tensor("xn_sc", [128, D], BF16))
        wqs_sb = E(nc.sbuf_tensor("wqs_sb", [128, NG * 128], BF16))
        wk_sb = E(nc.sbuf_tensor("wk_sb", [128, NG * 128], BF16))
        wv_sb = E(nc.sbuf_tensor("wv_sb", [128, NG * 128], BF16))
        bqs_sb = E(nc.sbuf_tensor("bqs_sb", [128, 1], F32))
        bk_sb = E(nc.sbuf_tensor("bk_sb", [128, 1], F32))
        bv_sb = E(nc.sbuf_tensor("bv_sb", [128, 1], F32))
        st1 = E(nc.sbuf_tensor("st1", [128, NTT], F32))
        st2 = E(nc.sbuf_tensor("st2", [128, NTT], F32))
        negm = E(nc.sbuf_tensor("negm", [128, NTT], F32))
        rstd = E(nc.sbuf_tensor("rstd", [128, NTT], F32))
        tmp1 = E(nc.sbuf_tensor("tmp1", [128, NTT], F32))
        tmp2 = E(nc.sbuf_tensor("tmp2", [128, NTT], F32))

        tape.enabled = do_prep
        with contextlib.ExitStack() as ap_:
            Ea = ap_.enter_context
            wqr = [Ea(nc.sbuf_tensor(f"wqr{i}", [128, 4096], F32))
                   for i in range(2)]
            wqt = Ea(nc.sbuf_tensor("wqt", [128, 128], F32))
            red_done = {}
            for g in range(NG):
                b = g % 2
                w = [(s_vec, red_done[g - 2])] if g - 2 in red_done else []
                if g < 2:
                    w.extend(s0_fence)
                tape.emit("sync", "dma_start", out=wqr[b][:, :],
                          in_=wq_in[g * 128:(g + 1) * 128, :],
                          waits=w, inc=(p_w[b], 16))
                ld = tape.val(p_w[b])
                tape.emit("vector", "reduce_sum", wqt[:, :],
                          wqr[b][:, :].rearrange("p (d h) -> p d h", h=32),
                          axis=AX.X, waits=[(p_w[b], ld)])
                tape.emit("vector", "tensor_copy",
                          wqs_sb[:, g * 128:(g + 1) * 128], wqt[:, :],
                          inc=(s_vec, 1))
                red_done[g] = tape.val(s_vec)
            for src, dst in [(wk_in, wk_sb), (wv_in, wv_sb)]:
                for g in range(NG):
                    b = g % 2
                    w = [(s_vec, tape.val(s_vec))]
                    tape.emit("sync", "dma_start", out=wqr[b][:, 0:128],
                              in_=src[g * 128:(g + 1) * 128, :],
                              waits=w, inc=(p_w[b], 16))
                    ld = tape.val(p_w[b])
                    tape.emit("vector", "tensor_copy",
                              dst[:, g * 128:(g + 1) * 128], wqr[b][:, 0:128],
                              waits=[(p_w[b], ld)], inc=(s_vec, 1))
            tape.emit("gpsimd", "dma_start", out=wqr[0][:, 0:32], in_=bq_in,
                      waits=[(s_vec, tape.val(s_vec))] + list(s0_fence),
                      inc=(p_b, 16))
            tape.emit("gpsimd", "dma_start", out=bk_sb[:, :], in_=bk_in,
                      inc=(p_b, 16))
            tape.emit("gpsimd", "dma_start", out=bv_sb[:, :], in_=bv_in,
                      inc=(p_b, 16))
            bld = tape.val(p_b)
            tape.emit("vector", "reduce_sum", bqs_sb[:, :], wqr[0][:, 0:32],
                      axis=AX.X, waits=[(p_b, bld)], inc=(s_vec, 1))
        tape.enabled = True
        prep_done = tape.val(s_vec)

        ln_evr = {}
        xnT3 = xnT[:, :].rearrange("p (g t) -> p g t", t=T_SUP)

        def plan_ln(tag, x_wait, dump_ap, pt_b):
            sq_rec = {}
            for tt in range(NTT):
                xt = x_sb[:, tt * D:(tt + 1) * D]
                tape.emit("vector", "reduce_sum", st1[:, tt:tt + 1], xt,
                          axis=AX.X, waits=x_wait if tt == 0 else (),
                          inc=(s_vec, 1))
                tape.emit("scalar", "activation", dump_ap, xt, AF.Square,
                          accum_out=st2[:, tt:tt + 1],
                          waits=x_wait if tt == 0 else (), inc=(s_sca, 1))
                sq_rec[tt] = tape.val(s_sca)
            sca_xnT = None
            tp_prev = None
            for tt in range(NTT):
                xt = x_sb[:, tt * D:(tt + 1) * D]
                c1 = st1[:, tt:tt + 1]
                c2 = st2[:, tt:tt + 1]
                cm = negm[:, tt:tt + 1]
                cr = rstd[:, tt:tt + 1]
                ca = tmp1[:, tt:tt + 1]
                cb = tmp2[:, tt:tt + 1]
                tape.emit("vector", "tensor_scalar", cm, c1, -1.0 / D, None,
                          OP.mult)
                tape.emit("vector", "tensor_scalar", ca, c2, 1.0 / D, None,
                          OP.mult, waits=[(s_sca, sq_rec[tt])])
                tape.emit("vector", "tensor_tensor", cb, cm, cm, OP.mult)
                tape.emit("vector", "tensor_tensor", ca, ca, cb, OP.subtract)
                tape.emit("vector", "tensor_scalar", ca, ca, EPS, None, OP.add,
                          inc=(s_vec, 1))
                vv = tape.val(s_vec)
                tape.emit("scalar", "sqrt", cb, ca, waits=[(s_vec, vv)],
                          inc=(s_sca, 1))
                sv = tape.val(s_sca)
                tape.emit("vector", "reciprocal", cr, cb, waits=[(s_sca, sv)])
                w = [(s_pe, tp_prev)] if tp_prev is not None else []
                tape.emit("vector", "tensor_scalar", xn_sc[:, :], xt, cm, cr,
                          OP.add, OP.mult, waits=w, inc=(s_vec, 1))
                xn_v = tape.val(s_vec)
                for pkk in range(4):
                    bk = pkk % 2
                    fw = [(s_vec, xn_v)] if pkk == 0 else []
                    kprev = (tag, tt, pkk - 2) if pkk >= 2 else \
                        (tag, tt - 1, pkk + 2)
                    if kprev in ln_evr:
                        fw.append((s_sca, ln_evr[kprev]))
                    g0 = pkk * 8
                    for j in range(8):
                        g = g0 + j
                        tape.emit("tensor", "transpose",
                                  pt_b[bk][:, j * 128:(j + 1) * 128],
                                  xn_sc[:, g * 128:(g + 1) * 128],
                                  eye_sb[:, :], waits=fw if j == 0 else (),
                                  inc=(s_pe, 1) if j == 7 else None)
                    pe_v = tape.val(s_pe)
                    if pkk == 3:
                        tp_prev = pe_v
                    dst = xnT3[:, g0:g0 + 8, tt * 128:(tt + 1) * 128]
                    src = pt_b[bk][:, :].rearrange("p (a c) -> p a c", c=128)
                    tape.emit("scalar", "copy", dst, src,
                              waits=[(s_pe, pe_v)], inc=(s_sca, 1))
                    ln_evr[(tag, tt, pkk)] = tape.val(s_sca)
                    sca_xnT = tape.val(s_sca)
            return sca_xnT

        store_recs = []
        first_wload = True
        for sup in range(N_SUP):
            tape.enabled = sup < n_sup
            for tt in range(NTT):
                w = []
                if tt == 0 and store_recs:
                    w = [(s_st, store_recs[-1])]
                if tt == 0 and not store_recs:
                    w = list(s0_fence)
                    if dbg_scratch is not None:
                        w.append((s_st, dbg_gps))
                tape.emit("sync", "dma_start",
                          out=x_sb[:, tt * D:(tt + 1) * D],
                          in_=x_in[sup * T_SUP + tt * 128:
                                   sup * T_SUP + (tt + 1) * 128, :],
                          waits=w, inc=(s_x, 16))
            x_loaded = tape.val(s_x)
            if sup == 0:
                xw = [(s_x, x_loaded), (s_vec, prep_done)]
            else:
                xw = [(s_x, x_loaded)]

            for pass_id in range(3):
                tape.enabled = (sup < n_sup) and (pass_id in passes)
                tag = (sup, pass_id)
                if pass_id in (0, 2):
                    with contextlib.ExitStack() as ps:
                        Ep = ps.enter_context
                        wgb = [Ep(nc.sbuf_tensor(f"wgb{i}_{sup}_{pass_id}", [128, D], BF16))
                               for i in range(2)]
                        wub = [Ep(nc.sbuf_tensor(f"wub{i}_{sup}_{pass_id}", [128, D], BF16))
                               for i in range(2)]
                        wdb = [Ep(nc.sbuf_tensor(f"wdb{i}_{sup}_{pass_id}", [128, 512], BF16))
                               for i in range(2)]
                        sgb = [Ep(nc.sbuf_tensor(f"sgb{i}_{sup}_{pass_id}", [128, T_SUP], BF16))
                               for i in range(2)]
                        uub = [Ep(nc.sbuf_tensor(f"uub{i}_{sup}_{pass_id}", [128, T_SUP], BF16))
                               for i in range(2)]
                        hT = Ep(nc.sbuf_tensor(f"hT_{sup}_{pass_id}", [128, 22 * T_SUP], BF16))
                        with contextlib.ExitStack() as lns:
                            pt_b = [lns.enter_context(
                                nc.psum_tensor(f"pt{sup}{pass_id}{i}",
                                               [128, 1024], BF16))
                                for i in range(2)]
                            xnT_done = plan_ln(tag, xw, hT[:, 0:D], pt_b)
                        pg = [Ep(nc.psum_tensor(f"pg{i}_{sup}_{pass_id}", [128, T_SUP], F32))
                              for i in range(2)]
                        pu = [Ep(nc.psum_tensor(f"pu{i}_{sup}_{pass_id}", [128, T_SUP], F32))
                              for i in range(2)]
                        pd = [Ep(nc.psum_tensor(f"pd{i}_{sup}_{pass_id}", [128, 512], F32))
                              for i in range(4)]

                        flush_rec = tape.val(s_vec)
                        pass_fence = tape.val(s_vec)
                        first_pass_dma = True
                        mmA = {}
                        siluA = {}
                        hA = {}
                        bdone = None
                        dn_k = [0]
                        dfree = {}
                        for (q0, qlen) in QUARTERS:
                            for idx in range(qlen):
                                it = q0 + idx
                                b = it % 2
                                w = []
                                if it - 2 in mmA:
                                    w.append((s_pe, mmA[it - 2]))
                                if first_wload:
                                    w.extend(s0_fence)
                                if first_pass_dma:
                                    w.append((s_vec, pass_fence))
                                    first_pass_dma = False
                                tape.emit("sync", "dma_start",
                                          out=wgb[b][:, :], in_=wgT_s[it],
                                          waits=w, inc=(m_g[b], 16))
                                ldg = tape.val(m_g[b])
                                tape.emit("gpsimd", "dma_start",
                                          out=wub[b][:, :], in_=wuT_s[it],
                                          waits=w, inc=(m_u[b], 16))
                                first_wload = False
                                ldu = tape.val(m_u[b])
                                w = [(m_g[b], ldg), (m_u[b], ldu),
                                     (s_sca, xnT_done)]
                                if it - 2 in siluA:
                                    w.append((s_sca, siluA[it - 2]))
                                if it - 2 in hA:
                                    w.append((s_vec, hA[it - 2]))
                                for g in range(NG):
                                    tape.emit(
                                        "tensor", "matmul", pg[b][:, :],
                                        wgb[b][:, g * 128:(g + 1) * 128],
                                        xnT[:, g * T_SUP:(g + 1) * T_SUP],
                                        start=(g == 0), stop=(g == NG - 1),
                                        waits=w if g == 0 else (),
                                        inc=(s_pe, 1) if g == NG - 1 else None)
                                mmg = tape.val(s_pe)
                                for g in range(NG):
                                    tape.emit(
                                        "tensor", "matmul", pu[b][:, :],
                                        wub[b][:, g * 128:(g + 1) * 128],
                                        xnT[:, g * T_SUP:(g + 1) * T_SUP],
                                        start=(g == 0), stop=(g == NG - 1),
                                        inc=(s_pe, 1) if g == NG - 1 else None)
                                mmA[it] = tape.val(s_pe)
                                tape.emit("scalar", "activation",
                                          sgb[b][:, :], pg[b][:, :], AF.Silu,
                                          waits=[(s_pe, mmg)], inc=(s_sca, 1))
                                siluA[it] = tape.val(s_sca)
                                tape.emit("vector", "tensor_copy",
                                          uub[b][:, :], pu[b][:, :],
                                          waits=[(s_pe, mmA[it])])
                                w = [(s_sca, siluA[it])]
                                if idx == 0 and bdone is not None:
                                    w.append((s_pe, bdone))
                                tape.emit("vector", "tensor_tensor",
                                          hT[:, idx * T_SUP:(idx + 1) * T_SUP],
                                          sgb[b][:, :], uub[b][:, :], OP.mult,
                                          waits=w, inc=(s_vec, 1))
                                hA[it] = tape.val(s_vec)
                            hq_done = tape.val(s_vec)
                            for oc in range(OCN):
                                for idx in range(qlen):
                                    it = q0 + idx
                                    k = dn_k[0]
                                    dn_k[0] += 1
                                    slot = k % 2
                                    eng = "sync" if k % 2 == 0 else "gpsimd"
                                    dsem = m_b[slot]
                                    w = []
                                    if k - 2 in dfree:
                                        w.append((s_pe, dfree[k - 2]))
                                    tape.emit(eng, "dma_start",
                                              out=wdb[slot][:, :],
                                              in_=wdT_s[oc, it], waits=w,
                                              inc=(dsem, 16))
                                    ldd = tape.val(dsem)
                                    w = [(dsem, ldd)]
                                    if idx == 0:
                                        w.append((s_vec, hq_done))
                                        w.append((s_vec, flush_rec))
                                    for tc in range(NTT):
                                        tape.emit(
                                            "tensor", "matmul", pd[tc][:, :],
                                            hT[:, idx * T_SUP + tc * 128:
                                               idx * T_SUP + (tc + 1) * 128],
                                            wdb[slot][:, :],
                                            start=(idx == 0),
                                            stop=(idx == qlen - 1),
                                            waits=w if tc == 0 else (),
                                            inc=(s_pe, 1) if tc == NTT - 1
                                            else None)
                                    dfree[k] = tape.val(s_pe)
                                mmd = tape.val(s_pe)
                                for tc in range(NTT):
                                    xv = x_sb[:, tc * D + oc * 512:
                                              tc * D + (oc + 1) * 512]
                                    tape.emit(
                                        "vector", "tensor_tensor", xv, xv,
                                        pd[tc][:, :], OP.add,
                                        waits=[(s_pe, mmd)] if tc == 0 else (),
                                        inc=(s_vec, 1) if tc == NTT - 1
                                        else None)
                                flush_rec = tape.val(s_vec)
                            bdone = tape.val(s_pe)
                    x_fin = tape.val(s_vec)
                    xw = [(s_vec, x_fin)]
                else:
                    with contextlib.ExitStack() as ps:
                        Ep = ps.enter_context
                        qsf = Ep(nc.sbuf_tensor(f"qsf_{sup}", [128, T_SUP], F32))
                        kf = Ep(nc.sbuf_tensor(f"kf_{sup}", [128, T_SUP], F32))
                        vf = Ep(nc.sbuf_tensor(f"vf_{sup}", [128, T_SUP], F32))
                        qtd = Ep(nc.sbuf_tensor(f"qtd_{sup}", [128, T_SUP], F32))
                        ktd = Ep(nc.sbuf_tensor(f"ktd_{sup}", [128, T_SUP], F32))
                        vtd = Ep(nc.sbuf_tensor(f"vtd_{sup}", [128, T_SUP], F32))
                        S_q = Ep(nc.sbuf_tensor(f"S_q_{sup}", [128, D], F32))
                        E_q = Ep(nc.sbuf_tensor(f"E_q_{sup}", [128, D], BF16))
                        dens = Ep(nc.sbuf_tensor(f"dens_{sup}", [128, 128], F32))
                        nums = Ep(nc.sbuf_tensor(f"nums_{sup}", [128, 128], F32))
                        rden = Ep(nc.sbuf_tensor(f"rden_{sup}", [128, 128], F32))
                        o_sb = Ep(nc.sbuf_tensor(f"o_sb_{sup}", [128, 128], F32))
                        with contextlib.ExitStack() as lns:
                            pt_b = [lns.enter_context(
                                nc.psum_tensor(f"pa{sup}{i}", [128, 1024],
                                               BF16)) for i in range(2)]
                            xnT_done = plan_ln(tag, xw, S_q[:, :], pt_b)
                        pq = Ep(nc.psum_tensor(f"pq_{sup}", [128, T_SUP], F32))
                        pkp = Ep(nc.psum_tensor(f"pkp_{sup}", [128, T_SUP], F32))
                        pv = Ep(nc.psum_tensor(f"pv_{sup}", [128, T_SUP], F32))
                        ptr = [Ep(nc.psum_tensor(f"ptr{i}_{sup}", [128, 512], F32))
                               for i in range(2)]

                        mm_rec = {}
                        for vi, (wsb, ps_) in enumerate(
                                [(wqs_sb, pq), (wk_sb, pkp), (wv_sb, pv)]):
                            w = [(s_sca, xnT_done)] if vi == 0 else []
                            for g in range(NG):
                                tape.emit(
                                    "tensor", "matmul", ps_[:, :],
                                    wsb[:, g * 128:(g + 1) * 128],
                                    xnT[:, g * T_SUP:(g + 1) * T_SUP],
                                    start=(g == 0), stop=(g == NG - 1),
                                    waits=w if g == 0 else (),
                                    inc=(s_pe, 1) if g == NG - 1 else None)
                            mm_rec[vi] = tape.val(s_pe)
                        tape.emit("vector", "tensor_scalar", qsf[:, :],
                                  pq[:, :], bqs_sb[:, 0:1], SC, OP.add,
                                  OP.mult, waits=[(s_pe, mm_rec[0])])
                        tape.emit("vector", "tensor_scalar", kf[:, :],
                                  pkp[:, :], bk_sb[:, 0:1], None, OP.add,
                                  waits=[(s_pe, mm_rec[1])])
                        tape.emit("vector", "tensor_scalar", vf[:, :],
                                  pv[:, :], bv_sb[:, 0:1], None, OP.add,
                                  waits=[(s_pe, mm_rec[2])], inc=(s_vec, 1))
                        ev_qkv = tape.val(s_vec)
                        td_rec = {}
                        for vi, (srcf, dstt) in enumerate(
                                [(qsf, qtd), (kf, ktd), (vf, vtd)]):
                            bank = ptr[vi % 2]
                            w = [(s_vec, ev_qkv)] if vi == 0 else []
                            if vi >= 2:
                                w.append((s_sca, td_rec[vi - 2]))
                            for tt in range(NTT):
                                tape.emit(
                                    "tensor", "transpose",
                                    bank[:, tt * 128:(tt + 1) * 128],
                                    srcf[:, tt * 128:(tt + 1) * 128],
                                    eye_fb[:, :], waits=w if tt == 0 else (),
                                    inc=(s_pe, 1) if tt == NTT - 1 else None)
                            pe_v = tape.val(s_pe)
                            tape.emit("scalar", "copy", dstt[:, :],
                                      bank[:, :], waits=[(s_pe, pe_v)],
                                      inc=(s_sca, 1))
                            td_rec[vi] = tape.val(s_sca)
                        td_all = tape.val(s_sca)

                        exp_rec = {}
                        red_rec = {}
                        for tt in range(NTT):
                            for dq_ in range(4):
                                kview = ktd[:, tt * 128:(tt + 1) * 128]\
                                    .unsqueeze(1).broadcast_to((128, 32, 128))
                                vview = vtd[:, tt * 128:(tt + 1) * 128]\
                                    .unsqueeze(1).broadcast_to((128, 32, 128))
                                qview = qtd[:, tt * 128 + dq_ * 32:
                                            tt * 128 + dq_ * 32 + 32]\
                                    .unsqueeze(2).broadcast_to((128, 32, 128))
                                sview = S_q[:, :].rearrange(
                                    "p (a b) -> p a b", b=128)
                                eview = E_q[:, :].rearrange(
                                    "p (a b) -> p a b", b=128)
                                w = []
                                if tt == 0 and dq_ == 0:
                                    w.append((s_sca, td_all))
                                kk = (tt, dq_)
                                pkk2 = (tt, dq_ - 1) if dq_ > 0 else (tt - 1, 3)
                                if pkk2 in exp_rec:
                                    w.append((s_sca, exp_rec[pkk2]))
                                tape.emit("vector", "tensor_tensor", sview,
                                          qview, kview, OP.mult, waits=w,
                                          inc=(s_vec, 1))
                                sv = tape.val(s_vec)
                                w = [(s_vec, sv)]
                                if pkk2 in red_rec:
                                    w.append((s_vec, red_rec[pkk2]))
                                tape.emit("scalar", "activation", E_q[:, :],
                                          S_q[:, :], AF.Exp, waits=w,
                                          inc=(s_sca, 1))
                                exp_rec[kk] = tape.val(s_sca)
                                tape.emit("vector", "reduce_sum",
                                          dens[:, dq_ * 32:(dq_ + 1) * 32],
                                          eview, axis=AX.X,
                                          waits=[(s_sca, exp_rec[kk])])
                                tape.emit("vector", "tensor_tensor", eview,
                                          eview, vview, OP.mult)
                                tape.emit("vector", "reduce_sum",
                                          nums[:, dq_ * 32:(dq_ + 1) * 32],
                                          eview, axis=AX.X, inc=(s_vec, 1))
                                red_rec[kk] = tape.val(s_vec)
                            tape.emit("vector", "reciprocal", rden[:, :],
                                      dens[:, :])
                            tape.emit("vector", "tensor_tensor", o_sb[:, :],
                                      nums[:, :], rden[:, :], OP.mult)
                            xv = x_sb[:, tt * D:(tt + 1) * D].rearrange(
                                "p (d h) -> p d h", h=32)
                            ov = o_sb[:, :].unsqueeze(2).broadcast_to(
                                (128, 128, 32))
                            tape.emit("vector", "tensor_tensor", xv, xv, ov,
                                      OP.add, inc=(s_vec, 1))
                    x_fin = tape.val(s_vec)
                    xw = [(s_vec, x_fin)]

            tape.enabled = sup < n_sup
            for tt in range(NTT):
                tape.emit("gpsimd", "dma_start",
                          out=out[sup * T_SUP + tt * 128:
                                  sup * T_SUP + (tt + 1) * 128, :],
                          in_=x_sb[:, tt * D:(tt + 1) * D],
                          waits=[(s_vec, x_fin)] if tt == 0 else (),
                          inc=(s_st, 16))
            store_recs.append(tape.val(s_st))

        tape.enabled = True
        final_gps = tape.val(s_st)

        with nc.Block() as block:
            def mk(eng_name):
                def f(eng):
                    for waits, method, args, kwargs, inc in tape.ops[eng_name]:
                        for (sem, v) in waits:
                            eng.wait_ge(sem, v)
                        inst = getattr(eng, method)(*args, **kwargs)
                        if inc is not None:
                            inst.then_inc(inc[0], inc[1])
                    if eng_name == "gpsimd":
                        eng.wait_ge(s_st, final_gps)
                f.__name__ = "_"
                return f

            block.sync(mk("sync"))
            block.tensor(mk("tensor"))
            block.vector(mk("vector"))
            block.scalar(mk("scalar"))
            block.gpsimd(mk("gpsimd"))

    return nc


_EYE_B = np.eye(128).astype(ml_dtypes.bfloat16)
_EYE_F = np.eye(128).astype(np.float32)


def kernel(L0_attn_x, L0_attn_Wq, L0_attn_bq, L0_attn_Wk, L0_attn_bk,
           L0_attn_Wv, L0_attn_bv, L0_ffn_qw_gate, L0_ffn_qw_up,
           L0_ffn_qw_down, L0_ffn_s_gate, L0_ffn_z_gate, L0_ffn_s_up,
           L0_ffn_z_up, L0_ffn_s_down, L0_ffn_z_down):
    x = np.ascontiguousarray(np.asarray(L0_attn_x).reshape(8192, D))
    nc = build_nc()
    shared = {
        "qwg": np.ascontiguousarray(L0_ffn_qw_gate),
        "qwu": np.ascontiguousarray(L0_ffn_qw_up),
        "qwd": np.ascontiguousarray(L0_ffn_qw_down),
        "sg": np.ascontiguousarray(L0_ffn_s_gate),
        "zg": np.ascontiguousarray(L0_ffn_z_gate),
        "su": np.ascontiguousarray(L0_ffn_s_up),
        "zu": np.ascontiguousarray(L0_ffn_z_up),
        "sd": np.ascontiguousarray(L0_ffn_s_down),
        "zd": np.ascontiguousarray(L0_ffn_z_down),
        "wq": np.ascontiguousarray(np.asarray(L0_attn_Wq).reshape(D, 128 * 32)),
        "bq": np.ascontiguousarray(L0_attn_bq),
        "wk": np.ascontiguousarray(np.asarray(L0_attn_Wk).reshape(D, 128)),
        "bk": np.ascontiguousarray(L0_attn_bk),
        "wv": np.ascontiguousarray(np.asarray(L0_attn_Wv).reshape(D, 128)),
        "bv": np.ascontiguousarray(L0_attn_bv),
        "eye_b": _EYE_B,
        "eye_f": _EYE_F,
    }
    in_maps = []
    for c in range(NCORES):
        m = dict(shared)
        m["x"] = np.ascontiguousarray(x[c * TOK:(c + 1) * TOK])
        in_maps.append(m)
    res = run_bass_kernel_spmd(nc, in_maps, core_ids=list(range(NCORES)))
    outs = [np.asarray(r["out"]) for r in res.results]
    return np.concatenate(outs, axis=0).reshape(4, 2048, D).astype(np.float32)


if __name__ == "__main__":
    build_nc()
    print("built ok")


# version 03
# speedup vs baseline: 1.3465x; 1.3465x over previous
import contextlib

import numpy as np
import ml_dtypes

import concourse.bass as bass
import concourse.mybir as mybir
from concourse.alu_op_type import AluOpType
from concourse.bass_utils import run_bass_kernel_spmd

F32 = mybir.dt.float32
BF16 = mybir.dt.bfloat16
I32 = mybir.dt.int32
AF = mybir.ActivationFunctionType
AX = mybir.AxisListType
OP = AluOpType

D = 4096
I_DIM = 11008
IT = 86
NG = 32
NCORES = 8
TOK = 1024
T_SUP = 512
N_SUP = 2
NTT = 4
OCN = 8
SC = float(1.0 / np.sqrt(128.0))
EPS = 1e-5
QUARTERS = [(0, 22), (22, 22), (44, 21), (65, 21)]
QUADS = [(q * 4, 4) for q in range(21)] + [(84, 2)]

ENGS = ("sync", "tensor", "vector", "scalar", "gpsimd")


class Tape:
    def __init__(self):
        self.ops = {e: [] for e in ENGS}
        self.cnt = {}
        self.enabled = True

    def emit(self, eng, method, *args, waits=(), inc=None, **kwargs):
        if not self.enabled:
            return
        if inc is not None:
            s, amt = inc
            self.cnt[s] = self.cnt.get(s, 0) + amt
        self.ops[eng].append((tuple(waits), method, args, kwargs, inc))

    def val(self, sem):
        return self.cnt.get(sem, 0)


def build_nc(do_stage0=True, do_prep=True, n_sup=N_SUP, passes=(0, 1, 2),
             dbg_scratch=None):
    nc = bass.Bass("TRN2", target_bir_lowering=False,
                   detect_race_conditions=False)

    def inp(name, shape, dt):
        return nc.dram_tensor(name, list(shape), dt, kind="ExternalInput").ap()

    x_in = inp("x", (TOK, D), F32)
    qwg = inp("qwg", (I_DIM, D), I32)
    qwu = inp("qwu", (I_DIM, D), I32)
    qwd = inp("qwd", (D, I_DIM), I32)
    sg_in = inp("sg", (I_DIM, 32), F32)
    zg_in = inp("zg", (I_DIM, 32), F32)
    su_in = inp("su", (I_DIM, 32), F32)
    zu_in = inp("zu", (I_DIM, 32), F32)
    sd_in = inp("sd", (D, IT), F32)
    zd_in = inp("zd", (D, IT), F32)
    wq_in = inp("wq", (D, 128 * 32), F32)
    bq_in = inp("bq", (128, 32), F32)
    wk_in = inp("wk", (D, 128), F32)
    bk_in = inp("bk", (128, 1), F32)
    wv_in = inp("wv", (D, 128), F32)
    bv_in = inp("bv", (128, 1), F32)
    eye_b = inp("eye_b", (128, 128), BF16)
    eye_f = inp("eye_f", (128, 128), F32)
    out = nc.dram_tensor("out", [TOK, D], F32, kind="ExternalOutput").ap()

    wgT_s = nc.dram_tensor("wgT_s", [IT, 128, D], BF16, kind="Internal").ap()
    wuT_s = nc.dram_tensor("wuT_s", [IT, 128, D], BF16, kind="Internal").ap()
    wdT_s = nc.dram_tensor("wdT_s", [OCN, IT, 128, 512], BF16, kind="Internal").ap()

    tape = Tape()
    with contextlib.ExitStack() as top:
        E = top.enter_context
        s_pe = E(nc.semaphore("s_pe"))
        s_vec = E(nc.semaphore("s_vec"))
        s_sca = E(nc.semaphore("s_sca"))
        s_eye = E(nc.semaphore("s_eye"))
        s0_q = [E(nc.semaphore(f"s0_q{i}")) for i in range(2)]
        s0_d = [E(nc.semaphore(f"s0_d{i}")) for i in range(2)]
        s0_z = E(nc.semaphore("s0_z"))
        s0_w = [E(nc.semaphore(f"s0_w{i}")) for i in range(2)]
        s0_y = [E(nc.semaphore(f"s0_y{i}")) for i in range(2)]
        p_w = [E(nc.semaphore(f"p_w{i}")) for i in range(2)]
        p_b = E(nc.semaphore("p_b"))
        s_x = E(nc.semaphore("s_x"))
        m_g = [E(nc.semaphore(f"m_g{i}")) for i in range(2)]
        m_u = [E(nc.semaphore(f"m_u{i}")) for i in range(2)]
        m_b = [E(nc.semaphore(f"m_b{i}")) for i in range(2)]
        s_st = E(nc.semaphore("s_st"))

        eye_sb = E(nc.sbuf_tensor("eye_sb", [128, 128], BF16))
        eye_fb = E(nc.sbuf_tensor("eye_fb", [128, 128], F32))
        tape.emit("sync", "dma_start", out=eye_sb[:, :], in_=eye_b, inc=(s_eye, 16))
        tape.emit("sync", "dma_start", out=eye_fb[:, :], in_=eye_f, inc=(s_eye, 16))

        tape.enabled = do_stage0
        with contextlib.ExitStack() as s0:
            E0 = s0.enter_context
            qb = [E0(nc.sbuf_tensor(f"qb{i}", [128, D], I32)) for i in range(2)]
            zb = [E0(nc.sbuf_tensor(f"zb{i}", [128, 32], F32)) for i in range(2)]
            sbu = [E0(nc.sbuf_tensor(f"sbu{i}", [128, 32], F32)) for i in range(2)]
            dqb = [E0(nc.sbuf_tensor(f"dqb{i}", [128, D], BF16)) for i in range(2)]
            asm = [E0(nc.sbuf_tensor(f"asm{i}", [128, D], BF16)) for i in range(2)]
            ptS = [E0(nc.psum_tensor(f"ptS{i}", [128, 1024], BF16)) for i in range(2)]

            bank_free = {0: None, 1: None}
            bank_id = [0]

            def do_pack(src_tiles, dst_ap, first_waits):
                bk = bank_id[0] % 2
                bank_id[0] += 1
                w = list(first_waits)
                if bank_free[bk] is not None:
                    w.append((s_sca, bank_free[bk]))
                n = len(src_tiles)
                for j, sap in enumerate(src_tiles):
                    tape.emit("tensor", "transpose",
                              ptS[bk][:, j * 128:(j + 1) * 128], sap,
                              eye_sb[:, :], waits=w if j == 0 else (),
                              inc=(s_pe, 1) if j == n - 1 else None)
                pe_v = tape.val(s_pe)
                if dst_ap.ndim == 4:
                    qn = dst_ap.shape[2]
                    src = ptS[bk][:, 0:n * 128].rearrange(
                        "p (a b c) -> p a b c", b=qn, c=128)
                else:
                    src = ptS[bk][:, 0:n * 128].rearrange(
                        "p (a c) -> p a c", c=128)
                tape.emit("scalar", "copy", dst_ap, src,
                          waits=[(s_pe, pe_v)], inc=(s_sca, 1))
                bank_free[bk] = tape.val(s_sca)
                return pe_v

            dq_done = {}
            tr_done = {}
            out_done = {}
            for mi, (qw, z_i, s_i, scr) in enumerate(
                    [(qwg, zg_in, sg_in, wgT_s), (qwu, zu_in, su_in, wuT_s)]):
                for it in range(IT):
                    b = it % 2
                    key = (mi, it)
                    pkey = (mi, it - 2) if it >= 2 else (mi - 1, IT - 2 + it)
                    w = []
                    if pkey in dq_done:
                        w.append((s_vec, dq_done[pkey]))
                    tape.emit("sync", "dma_start", out=qb[b][:, :],
                              in_=qw[it * 128:(it + 1) * 128, :],
                              waits=w, inc=(s0_q[b], 16))
                    tape.emit("sync", "dma_start", out=zb[b][:, :],
                              in_=z_i[it * 128:(it + 1) * 128, :],
                              inc=(s0_q[b], 16))
                    tape.emit("sync", "dma_start", out=sbu[b][:, :],
                              in_=s_i[it * 128:(it + 1) * 128, :],
                              inc=(s0_q[b], 16))
                    ld = tape.val(s0_q[b])
                    w = [(s0_q[b], ld)]
                    if pkey in tr_done:
                        w.append((s_pe, tr_done[pkey]))
                    for g in range(NG):
                        tape.emit("vector", "tensor_scalar",
                                  dqb[b][:, g * 128:(g + 1) * 128],
                                  qb[b][:, g * 128:(g + 1) * 128],
                                  zb[b][:, g:g + 1], sbu[b][:, g:g + 1],
                                  OP.subtract, OP.mult,
                                  waits=w if g == 0 else (),
                                  inc=(s_vec, 1) if g == NG - 1 else None)
                    dq_done[key] = tape.val(s_vec)
                    last_pe = None
                    for pkk in range(4):
                        fw = [(s_vec, dq_done[key])] if pkk == 0 else []
                        if pkk == 0 and it == 0 and mi == 0:
                            fw.append((s_eye, 32))
                        if pkk == 0 and pkey in out_done:
                            fw.append((s0_w[b], out_done[pkey]))
                        tiles = [dqb[b][:, g * 128:(g + 1) * 128]
                                 for g in range(pkk * 8, pkk * 8 + 8)]
                        dst = asm[b][:, pkk * 1024:(pkk + 1) * 1024]\
                            .rearrange("p (a c) -> p a c", c=128)
                        last_pe = do_pack(tiles, dst, fw)
                    tr_done[key] = last_pe
                    tape.emit("scalar", "dma_start", out=scr[it],
                              in_=asm[b][:, :],
                              waits=[(s_sca, bank_free[(bank_id[0] - 1) % 2])],
                              inc=(s0_w[b], 16))
                    out_done[key] = tape.val(s0_w[b])

            sdh = E0(nc.sbuf_tensor("sdh", [128, 32 * IT], F32))
            zdx = E0(nc.sbuf_tensor("zdx", [128, 32 * IT], F32))
            sd_r = sd_in.rearrange("(ot p) g -> p ot g", p=128)
            zd_r = zd_in.rearrange("(ot p) g -> p ot g", p=128)
            sdh3 = sdh[:, :].rearrange("p (ot g) -> p ot g", g=IT)
            zdx3 = zdx[:, :].rearrange("p (ot g) -> p ot g", g=IT)
            tape.emit("gpsimd", "dma_start", out=zdx3, in_=zd_r,
                      inc=(s0_z, 16))
            tape.emit("gpsimd", "dma_start", out=sdh3, in_=sd_r,
                      inc=(s0_z, 16))
            tape.emit("vector", "tensor_scalar", sdh[:, :], sdh[:, :], 0.5,
                      None, OP.mult, waits=[(s0_z, 32)], inc=(s_vec, 1))
            sdh_done = tape.val(s_vec)

            qd = [E0(nc.sbuf_tensor(f"qd{i}", [128, 512], I32)) for i in range(2)]
            dqd = [E0(nc.sbuf_tensor(f"dqd{i}", [128, 512], BF16)) for i in range(2)]
            asm4 = [E0(nc.sbuf_tensor(f"asm4_{i}", [128, 4 * D], BF16))
                    for i in range(2)]

            dq_dn = {}
            tr_dn = {}
            out_dn = {}
            for qi, (it0, qlen) in enumerate(QUADS):
                ab = qi % 2
                nper = 2
                pend = []
                first_evac = True
                for ot in range(32):
                    slot = ot % 2
                    kprev = (qi, ot - 2) if ot >= 2 else (qi - 1, 30 + ot)
                    w = []
                    if kprev in dq_dn:
                        w.append((s_vec, dq_dn[kprev]))
                    tape.emit("sync", "dma_start",
                              out=qd[slot][:, 0:qlen * 128],
                              in_=qwd[ot * 128:(ot + 1) * 128,
                                      it0 * 128:(it0 + qlen) * 128],
                              waits=w, inc=(s0_d[slot], 16))
                    ldv = tape.val(s0_d[slot])
                    w = [(s0_d[slot], ldv)]
                    if qi == 0 and ot == 0:
                        w.append((s_vec, sdh_done))
                    if kprev in tr_dn:
                        w.append((s_pe, tr_dn[kprev]))
                    for itl in range(qlen):
                        g = it0 + itl
                        tape.emit("vector", "tensor_scalar",
                                  dqd[slot][:, itl * 128:(itl + 1) * 128],
                                  qd[slot][:, itl * 128:(itl + 1) * 128],
                                  zdx[:, ot * IT + g:ot * IT + g + 1],
                                  sdh[:, ot * IT + g:ot * IT + g + 1],
                                  OP.subtract, OP.mult,
                                  waits=w if itl == 0 else (),
                                  inc=(s_vec, 1) if itl == qlen - 1 else None)
                    dq_dn[(qi, ot)] = tape.val(s_vec)
                    pend.append((ot, slot, tape.val(s_vec)))
                    if len(pend) == nper:
                        tiles = []
                        fw = [(s_vec, dq_dn[(qi, ot)])]
                        if first_evac and (qi - 2) in out_dn:
                            fw.append((s0_y[ab], out_dn[qi - 2]))
                        for (ot_, sl_, _) in pend:
                            for itl in range(qlen):
                                tiles.append(
                                    dqd[sl_][:, itl * 128:(itl + 1) * 128])
                        ot0 = pend[0][0]
                        dst = asm4[ab][:, :].rearrange(
                            "p (i o j) -> p o i j", o=32, j=128)[
                            :, ot0:ot0 + nper, 0:qlen, :]
                        pe_v = do_pack(tiles, dst, fw)
                        for (ot_, _, _) in pend:
                            tr_dn[(qi, ot_)] = pe_v
                        first_evac = False
                        pend = []
                w = [(s_sca, bank_free[(bank_id[0] - 1) % 2])]
                for itl in range(qlen):
                    for oc in range(OCN):
                        tape.emit("scalar", "dma_start",
                                  out=wdT_s[oc, it0 + itl],
                                  in_=asm4[ab][:, itl * D + oc * 512:
                                               itl * D + (oc + 1) * 512],
                                  waits=w if (itl == 0 and oc == 0) else (),
                                  inc=(s0_y[ab], 16))
                out_dn[qi] = tape.val(s0_y[ab])

        tape.enabled = True
        s0_fence = [(s0_w[0], tape.val(s0_w[0])), (s0_w[1], tape.val(s0_w[1])),
                    (s0_y[0], tape.val(s0_y[0])), (s0_y[1], tape.val(s0_y[1]))]

        if dbg_scratch is not None:
            dbg_out = nc.dram_tensor("dbg_out", [128, 3 * D], BF16,
                                     kind="ExternalOutput").ap()
            dbg_stk = contextlib.ExitStack()
            dbg_sb = dbg_stk.enter_context(
                nc.sbuf_tensor("dbg_sb", [128, 3 * D], BF16))
            it_d, oc_d = dbg_scratch
            tape.emit("sync", "dma_start", out=dbg_sb[:, 0:D],
                      in_=wgT_s[it_d], waits=list(s0_fence),
                      inc=(p_b, 16))
            tape.emit("sync", "dma_start", out=dbg_sb[:, D:2 * D],
                      in_=wuT_s[it_d], inc=(p_b, 16))
            for j in range(OCN):
                tape.emit("sync", "dma_start",
                          out=dbg_sb[:, 2 * D + j * 512:2 * D + (j + 1) * 512],
                          in_=wdT_s[j, oc_d], inc=(p_b, 16))
            dl = tape.val(p_b)
            tape.emit("gpsimd", "dma_start", out=dbg_out, in_=dbg_sb[:, :],
                      waits=[(p_b, dl)], inc=(s_st, 16))
            dbg_stk.close()
            dbg_gps = tape.val(s_st)

        x_sb = E(nc.sbuf_tensor("x_sb", [128, NTT * D], F32))
        xnT = E(nc.sbuf_tensor("xnT", [128, NG * T_SUP], BF16))
        xn_sc = E(nc.sbuf_tensor("xn_sc", [128, D], BF16))
        wqs_sb = E(nc.sbuf_tensor("wqs_sb", [128, NG * 128], BF16))
        wk_sb = E(nc.sbuf_tensor("wk_sb", [128, NG * 128], BF16))
        wv_sb = E(nc.sbuf_tensor("wv_sb", [128, NG * 128], BF16))
        bqs_sb = E(nc.sbuf_tensor("bqs_sb", [128, 1], F32))
        bk_sb = E(nc.sbuf_tensor("bk_sb", [128, 1], F32))
        bv_sb = E(nc.sbuf_tensor("bv_sb", [128, 1], F32))
        st1 = E(nc.sbuf_tensor("st1", [128, NTT], F32))
        st2 = E(nc.sbuf_tensor("st2", [128, NTT], F32))
        negm = E(nc.sbuf_tensor("negm", [128, NTT], F32))
        rstd = E(nc.sbuf_tensor("rstd", [128, NTT], F32))
        tmp1 = E(nc.sbuf_tensor("tmp1", [128, NTT], F32))
        tmp2 = E(nc.sbuf_tensor("tmp2", [128, NTT], F32))

        tape.enabled = do_prep
        with contextlib.ExitStack() as ap_:
            Ea = ap_.enter_context
            wqr = [Ea(nc.sbuf_tensor(f"wqr{i}", [128, 4096], F32))
                   for i in range(2)]
            wqt = Ea(nc.sbuf_tensor("wqt", [128, 128], F32))
            red_done = {}
            for g in range(NG):
                b = g % 2
                w = [(s_vec, red_done[g - 2])] if g - 2 in red_done else []
                if g < 2:
                    w.extend(s0_fence)
                tape.emit("sync", "dma_start", out=wqr[b][:, :],
                          in_=wq_in[g * 128:(g + 1) * 128, :],
                          waits=w, inc=(p_w[b], 16))
                ld = tape.val(p_w[b])
                tape.emit("vector", "reduce_sum", wqt[:, :],
                          wqr[b][:, :].rearrange("p (d h) -> p d h", h=32),
                          axis=AX.X, waits=[(p_w[b], ld)])
                tape.emit("vector", "tensor_copy",
                          wqs_sb[:, g * 128:(g + 1) * 128], wqt[:, :],
                          inc=(s_vec, 1))
                red_done[g] = tape.val(s_vec)
            for src, dst in [(wk_in, wk_sb), (wv_in, wv_sb)]:
                for g in range(NG):
                    b = g % 2
                    w = [(s_vec, tape.val(s_vec))]
                    tape.emit("sync", "dma_start", out=wqr[b][:, 0:128],
                              in_=src[g * 128:(g + 1) * 128, :],
                              waits=w, inc=(p_w[b], 16))
                    ld = tape.val(p_w[b])
                    tape.emit("vector", "tensor_copy",
                              dst[:, g * 128:(g + 1) * 128], wqr[b][:, 0:128],
                              waits=[(p_w[b], ld)], inc=(s_vec, 1))
            tape.emit("gpsimd", "dma_start", out=wqr[0][:, 0:32], in_=bq_in,
                      waits=[(s_vec, tape.val(s_vec))] + list(s0_fence),
                      inc=(p_b, 16))
            tape.emit("gpsimd", "dma_start", out=bk_sb[:, :], in_=bk_in,
                      inc=(p_b, 16))
            tape.emit("gpsimd", "dma_start", out=bv_sb[:, :], in_=bv_in,
                      inc=(p_b, 16))
            bld = tape.val(p_b)
            tape.emit("vector", "reduce_sum", bqs_sb[:, :], wqr[0][:, 0:32],
                      axis=AX.X, waits=[(p_b, bld)], inc=(s_vec, 1))
        tape.enabled = True
        prep_done = tape.val(s_vec)

        ln_evr = {}
        xnT3 = xnT[:, :].rearrange("p (g t) -> p g t", t=T_SUP)

        def plan_ln(tag, x_wait, dump_ap, pt_b):
            sq_rec = {}
            for tt in range(NTT):
                xt = x_sb[:, tt * D:(tt + 1) * D]
                tape.emit("vector", "reduce_sum", st1[:, tt:tt + 1], xt,
                          axis=AX.X, waits=x_wait if tt == 0 else (),
                          inc=(s_vec, 1))
                tape.emit("scalar", "activation", dump_ap, xt, AF.Square,
                          accum_out=st2[:, tt:tt + 1],
                          waits=x_wait if tt == 0 else (), inc=(s_sca, 1))
                sq_rec[tt] = tape.val(s_sca)
            sca_xnT = None
            tp_prev = None
            for tt in range(NTT):
                xt = x_sb[:, tt * D:(tt + 1) * D]
                c1 = st1[:, tt:tt + 1]
                c2 = st2[:, tt:tt + 1]
                cm = negm[:, tt:tt + 1]
                cr = rstd[:, tt:tt + 1]
                ca = tmp1[:, tt:tt + 1]
                cb = tmp2[:, tt:tt + 1]
                tape.emit("vector", "tensor_scalar", cm, c1, -1.0 / D, None,
                          OP.mult)
                tape.emit("vector", "tensor_scalar", ca, c2, 1.0 / D, None,
                          OP.mult, waits=[(s_sca, sq_rec[tt])])
                tape.emit("vector", "tensor_tensor", cb, cm, cm, OP.mult)
                tape.emit("vector", "tensor_tensor", ca, ca, cb, OP.subtract)
                tape.emit("vector", "tensor_scalar", ca, ca, EPS, None, OP.add,
                          inc=(s_vec, 1))
                vv = tape.val(s_vec)
                tape.emit("scalar", "sqrt", cb, ca, waits=[(s_vec, vv)],
                          inc=(s_sca, 1))
                sv = tape.val(s_sca)
                tape.emit("vector", "reciprocal", cr, cb, waits=[(s_sca, sv)])
                w = [(s_pe, tp_prev)] if tp_prev is not None else []
                tape.emit("vector", "tensor_scalar", xn_sc[:, :], xt, cm, cr,
                          OP.add, OP.mult, waits=w, inc=(s_vec, 1))
                xn_v = tape.val(s_vec)
                for pkk in range(4):
                    bk = pkk % 2
                    fw = [(s_vec, xn_v)] if pkk == 0 else []
                    kprev = (tag, tt, pkk - 2) if pkk >= 2 else \
                        (tag, tt - 1, pkk + 2)
                    if kprev in ln_evr:
                        fw.append((s_sca, ln_evr[kprev]))
                    g0 = pkk * 8
                    for j in range(8):
                        g = g0 + j
                        tape.emit("tensor", "transpose",
                                  pt_b[bk][:, j * 128:(j + 1) * 128],
                                  xn_sc[:, g * 128:(g + 1) * 128],
                                  eye_sb[:, :], waits=fw if j == 0 else (),
                                  inc=(s_pe, 1) if j == 7 else None)
                    pe_v = tape.val(s_pe)
                    if pkk == 3:
                        tp_prev = pe_v
                    dst = xnT3[:, g0:g0 + 8, tt * 128:(tt + 1) * 128]
                    src = pt_b[bk][:, :].rearrange("p (a c) -> p a c", c=128)
                    tape.emit("scalar", "copy", dst, src,
                              waits=[(s_pe, pe_v)], inc=(s_sca, 1))
                    ln_evr[(tag, tt, pkk)] = tape.val(s_sca)
                    sca_xnT = tape.val(s_sca)
            return sca_xnT

        store_recs = []
        first_wload = True
        for sup in range(N_SUP):
            tape.enabled = sup < n_sup
            for tt in range(NTT):
                w = []
                if tt == 0 and store_recs:
                    w = [(s_st, store_recs[-1])]
                if tt == 0 and not store_recs:
                    w = list(s0_fence)
                    if dbg_scratch is not None:
                        w.append((s_st, dbg_gps))
                tape.emit("sync", "dma_start",
                          out=x_sb[:, tt * D:(tt + 1) * D],
                          in_=x_in[sup * T_SUP + tt * 128:
                                   sup * T_SUP + (tt + 1) * 128, :],
                          waits=w, inc=(s_x, 16))
            x_loaded = tape.val(s_x)
            if sup == 0:
                xw = [(s_x, x_loaded), (s_vec, prep_done)]
            else:
                xw = [(s_x, x_loaded)]

            for pass_id in range(3):
                tape.enabled = (sup < n_sup) and (pass_id in passes)
                tag = (sup, pass_id)
                if pass_id in (0, 2):
                    with contextlib.ExitStack() as ps:
                        Ep = ps.enter_context
                        wgb = [Ep(nc.sbuf_tensor(f"wgb{i}_{sup}_{pass_id}", [128, D], BF16))
                               for i in range(2)]
                        wub = [Ep(nc.sbuf_tensor(f"wub{i}_{sup}_{pass_id}", [128, D], BF16))
                               for i in range(2)]
                        wdb = [Ep(nc.sbuf_tensor(f"wdb{i}_{sup}_{pass_id}",
                                                 [128, 4 * 512], BF16))
                               for i in range(2)]
                        sgb = [Ep(nc.sbuf_tensor(f"sgb{i}_{sup}_{pass_id}", [128, T_SUP], BF16))
                               for i in range(2)]
                        uub = [Ep(nc.sbuf_tensor(f"uub{i}_{sup}_{pass_id}", [128, T_SUP], BF16))
                               for i in range(2)]
                        hT = Ep(nc.sbuf_tensor(f"hT_{sup}_{pass_id}", [128, 22 * T_SUP], BF16))
                        with contextlib.ExitStack() as lns:
                            pt_b = [lns.enter_context(
                                nc.psum_tensor(f"pt{sup}{pass_id}{i}",
                                               [128, 1024], BF16))
                                for i in range(2)]
                            xnT_done = plan_ln(tag, xw, hT[:, 0:D], pt_b)
                        pg = [Ep(nc.psum_tensor(f"pg{i}_{sup}_{pass_id}", [128, T_SUP], F32))
                              for i in range(2)]
                        pu = [Ep(nc.psum_tensor(f"pu{i}_{sup}_{pass_id}", [128, T_SUP], F32))
                              for i in range(2)]
                        pd = [Ep(nc.psum_tensor(f"pd{i}_{sup}_{pass_id}", [128, 512], F32))
                              for i in range(4)]

                        flush_rec = tape.val(s_vec)
                        pass_fence = tape.val(s_vec)
                        first_pass_dma = True
                        mmA = {}
                        siluA = {}
                        hA = {}
                        bdone = None
                        dn_k = [0]
                        dfree = {}
                        for (q0, qlen) in QUARTERS:
                            for idx in range(qlen):
                                it = q0 + idx
                                b = it % 2
                                w = []
                                if it - 2 in mmA:
                                    w.append((s_pe, mmA[it - 2]))
                                if first_wload:
                                    w.extend(s0_fence)
                                if first_pass_dma:
                                    w.append((s_vec, pass_fence))
                                    first_pass_dma = False
                                tape.emit("sync", "dma_start",
                                          out=wgb[b][:, :], in_=wgT_s[it],
                                          waits=w, inc=(m_g[b], 16))
                                ldg = tape.val(m_g[b])
                                tape.emit("gpsimd", "dma_start",
                                          out=wub[b][:, :], in_=wuT_s[it],
                                          waits=w, inc=(m_u[b], 16))
                                first_wload = False
                                ldu = tape.val(m_u[b])
                                w = [(m_g[b], ldg), (m_u[b], ldu),
                                     (s_sca, xnT_done)]
                                if it - 2 in siluA:
                                    w.append((s_sca, siluA[it - 2]))
                                if it - 2 in hA:
                                    w.append((s_vec, hA[it - 2]))
                                for g in range(NG):
                                    tape.emit(
                                        "tensor", "matmul", pg[b][:, :],
                                        wgb[b][:, g * 128:(g + 1) * 128],
                                        xnT[:, g * T_SUP:(g + 1) * T_SUP],
                                        start=(g == 0), stop=(g == NG - 1),
                                        waits=w if g == 0 else (),
                                        inc=(s_pe, 1) if g == NG - 1 else None)
                                mmg = tape.val(s_pe)
                                for g in range(NG):
                                    tape.emit(
                                        "tensor", "matmul", pu[b][:, :],
                                        wub[b][:, g * 128:(g + 1) * 128],
                                        xnT[:, g * T_SUP:(g + 1) * T_SUP],
                                        start=(g == 0), stop=(g == NG - 1),
                                        inc=(s_pe, 1) if g == NG - 1 else None)
                                mmA[it] = tape.val(s_pe)
                                tape.emit("scalar", "activation",
                                          sgb[b][:, :], pg[b][:, :], AF.Silu,
                                          waits=[(s_pe, mmg)], inc=(s_sca, 1))
                                siluA[it] = tape.val(s_sca)
                                tape.emit("vector", "tensor_copy",
                                          uub[b][:, :], pu[b][:, :],
                                          waits=[(s_pe, mmA[it])])
                                w = [(s_sca, siluA[it])]
                                if idx == 0 and bdone is not None:
                                    w.append((s_pe, bdone))
                                tape.emit("vector", "tensor_tensor",
                                          hT[:, idx * T_SUP:(idx + 1) * T_SUP],
                                          sgb[b][:, :], uub[b][:, :], OP.mult,
                                          waits=w, inc=(s_vec, 1))
                                hA[it] = tape.val(s_vec)
                            hq_done = tape.val(s_vec)
                            chunks = [(c0, min(4, qlen - c0))
                                      for c0 in range(0, qlen, 4)]
                            for oc in range(OCN):
                                for (c0, cw) in chunks:
                                    it = q0 + c0
                                    k = dn_k[0]
                                    dn_k[0] += 1
                                    slot = k % 2
                                    eng = "sync" if k % 2 == 0 else "gpsimd"
                                    dsem = m_b[slot]
                                    w = []
                                    if k - 2 in dfree:
                                        w.append((s_pe, dfree[k - 2]))
                                    dst = wdb[slot][:, :].rearrange(
                                        "p (i c) -> p i c", c=512)[:, 0:cw, :]
                                    srcw = wdT_s[oc, it:it + cw]                                        .rearrange("i p c -> p i c")
                                    tape.emit(eng, "dma_start", out=dst,
                                              in_=srcw, waits=w,
                                              inc=(dsem, 16))
                                    ldd = tape.val(dsem)
                                    w = [(dsem, ldd)]
                                    if c0 == 0:
                                        w.append((s_vec, hq_done))
                                        w.append((s_vec, flush_rec))
                                    for j in range(cw):
                                        idx = c0 + j
                                        for tc in range(NTT):
                                            tape.emit(
                                                "tensor", "matmul",
                                                pd[tc][:, :],
                                                hT[:, idx * T_SUP + tc * 128:
                                                   idx * T_SUP + (tc + 1) * 128],
                                                wdb[slot][:, j * 512:
                                                          (j + 1) * 512],
                                                start=(idx == 0),
                                                stop=(idx == qlen - 1),
                                                waits=w if (tc == 0 and j == 0)
                                                else (),
                                                inc=(s_pe, 1)
                                                if (tc == NTT - 1 and j == cw - 1)
                                                else None)
                                    dfree[k] = tape.val(s_pe)
                                mmd = tape.val(s_pe)
                                for tc in range(NTT):
                                    xv = x_sb[:, tc * D + oc * 512:
                                              tc * D + (oc + 1) * 512]
                                    tape.emit(
                                        "vector", "tensor_tensor", xv, xv,
                                        pd[tc][:, :], OP.add,
                                        waits=[(s_pe, mmd)] if tc == 0 else (),
                                        inc=(s_vec, 1) if tc == NTT - 1
                                        else None)
                                flush_rec = tape.val(s_vec)
                            bdone = tape.val(s_pe)
                    x_fin = tape.val(s_vec)
                    xw = [(s_vec, x_fin)]
                else:
                    with contextlib.ExitStack() as ps:
                        Ep = ps.enter_context
                        qsf = Ep(nc.sbuf_tensor(f"qsf_{sup}", [128, T_SUP], F32))
                        kf = Ep(nc.sbuf_tensor(f"kf_{sup}", [128, T_SUP], F32))
                        vf = Ep(nc.sbuf_tensor(f"vf_{sup}", [128, T_SUP], F32))
                        qtd = Ep(nc.sbuf_tensor(f"qtd_{sup}", [128, T_SUP], F32))
                        ktd = Ep(nc.sbuf_tensor(f"ktd_{sup}", [128, T_SUP], F32))
                        vtd = Ep(nc.sbuf_tensor(f"vtd_{sup}", [128, T_SUP], F32))
                        S_q = Ep(nc.sbuf_tensor(f"S_q_{sup}", [128, D], F32))
                        E_q = Ep(nc.sbuf_tensor(f"E_q_{sup}", [128, D], BF16))
                        dens = Ep(nc.sbuf_tensor(f"dens_{sup}", [128, 128], F32))
                        nums = Ep(nc.sbuf_tensor(f"nums_{sup}", [128, 128], F32))
                        rden = Ep(nc.sbuf_tensor(f"rden_{sup}", [128, 128], F32))
                        o_sb = Ep(nc.sbuf_tensor(f"o_sb_{sup}", [128, 128], F32))
                        with contextlib.ExitStack() as lns:
                            pt_b = [lns.enter_context(
                                nc.psum_tensor(f"pa{sup}{i}", [128, 1024],
                                               BF16)) for i in range(2)]
                            xnT_done = plan_ln(tag, xw, S_q[:, :], pt_b)
                        pq = Ep(nc.psum_tensor(f"pq_{sup}", [128, T_SUP], F32))
                        pkp = Ep(nc.psum_tensor(f"pkp_{sup}", [128, T_SUP], F32))
                        pv = Ep(nc.psum_tensor(f"pv_{sup}", [128, T_SUP], F32))
                        ptr = [Ep(nc.psum_tensor(f"ptr{i}_{sup}", [128, 512], F32))
                               for i in range(2)]

                        mm_rec = {}
                        for vi, (wsb, ps_) in enumerate(
                                [(wqs_sb, pq), (wk_sb, pkp), (wv_sb, pv)]):
                            w = [(s_sca, xnT_done)] if vi == 0 else []
                            for g in range(NG):
                                tape.emit(
                                    "tensor", "matmul", ps_[:, :],
                                    wsb[:, g * 128:(g + 1) * 128],
                                    xnT[:, g * T_SUP:(g + 1) * T_SUP],
                                    start=(g == 0), stop=(g == NG - 1),
                                    waits=w if g == 0 else (),
                                    inc=(s_pe, 1) if g == NG - 1 else None)
                            mm_rec[vi] = tape.val(s_pe)
                        tape.emit("vector", "tensor_scalar", qsf[:, :],
                                  pq[:, :], bqs_sb[:, 0:1], SC, OP.add,
                                  OP.mult, waits=[(s_pe, mm_rec[0])])
                        tape.emit("vector", "tensor_scalar", kf[:, :],
                                  pkp[:, :], bk_sb[:, 0:1], None, OP.add,
                                  waits=[(s_pe, mm_rec[1])])
                        tape.emit("vector", "tensor_scalar", vf[:, :],
                                  pv[:, :], bv_sb[:, 0:1], None, OP.add,
                                  waits=[(s_pe, mm_rec[2])], inc=(s_vec, 1))
                        ev_qkv = tape.val(s_vec)
                        td_rec = {}
                        for vi, (srcf, dstt) in enumerate(
                                [(qsf, qtd), (kf, ktd), (vf, vtd)]):
                            bank = ptr[vi % 2]
                            w = [(s_vec, ev_qkv)] if vi == 0 else []
                            if vi >= 2:
                                w.append((s_sca, td_rec[vi - 2]))
                            for tt in range(NTT):
                                tape.emit(
                                    "tensor", "transpose",
                                    bank[:, tt * 128:(tt + 1) * 128],
                                    srcf[:, tt * 128:(tt + 1) * 128],
                                    eye_fb[:, :], waits=w if tt == 0 else (),
                                    inc=(s_pe, 1) if tt == NTT - 1 else None)
                            pe_v = tape.val(s_pe)
                            tape.emit("scalar", "copy", dstt[:, :],
                                      bank[:, :], waits=[(s_pe, pe_v)],
                                      inc=(s_sca, 1))
                            td_rec[vi] = tape.val(s_sca)
                        td_all = tape.val(s_sca)

                        exp_rec = {}
                        red_rec = {}
                        for tt in range(NTT):
                            for dq_ in range(4):
                                kview = ktd[:, tt * 128:(tt + 1) * 128]\
                                    .unsqueeze(1).broadcast_to((128, 32, 128))
                                vview = vtd[:, tt * 128:(tt + 1) * 128]\
                                    .unsqueeze(1).broadcast_to((128, 32, 128))
                                qview = qtd[:, tt * 128 + dq_ * 32:
                                            tt * 128 + dq_ * 32 + 32]\
                                    .unsqueeze(2).broadcast_to((128, 32, 128))
                                sview = S_q[:, :].rearrange(
                                    "p (a b) -> p a b", b=128)
                                eview = E_q[:, :].rearrange(
                                    "p (a b) -> p a b", b=128)
                                w = []
                                if tt == 0 and dq_ == 0:
                                    w.append((s_sca, td_all))
                                kk = (tt, dq_)
                                pkk2 = (tt, dq_ - 1) if dq_ > 0 else (tt - 1, 3)
                                if pkk2 in exp_rec:
                                    w.append((s_sca, exp_rec[pkk2]))
                                tape.emit("vector", "tensor_tensor", sview,
                                          qview, kview, OP.mult, waits=w,
                                          inc=(s_vec, 1))
                                sv = tape.val(s_vec)
                                w = [(s_vec, sv)]
                                if pkk2 in red_rec:
                                    w.append((s_vec, red_rec[pkk2]))
                                tape.emit("scalar", "activation", E_q[:, :],
                                          S_q[:, :], AF.Exp, waits=w,
                                          inc=(s_sca, 1))
                                exp_rec[kk] = tape.val(s_sca)
                                tape.emit("vector", "reduce_sum",
                                          dens[:, dq_ * 32:(dq_ + 1) * 32],
                                          eview, axis=AX.X,
                                          waits=[(s_sca, exp_rec[kk])])
                                tape.emit("vector", "tensor_tensor", eview,
                                          eview, vview, OP.mult)
                                tape.emit("vector", "reduce_sum",
                                          nums[:, dq_ * 32:(dq_ + 1) * 32],
                                          eview, axis=AX.X, inc=(s_vec, 1))
                                red_rec[kk] = tape.val(s_vec)
                            tape.emit("vector", "reciprocal", rden[:, :],
                                      dens[:, :])
                            tape.emit("vector", "tensor_tensor", o_sb[:, :],
                                      nums[:, :], rden[:, :], OP.mult)
                            xv = x_sb[:, tt * D:(tt + 1) * D].rearrange(
                                "p (d h) -> p d h", h=32)
                            ov = o_sb[:, :].unsqueeze(2).broadcast_to(
                                (128, 128, 32))
                            tape.emit("vector", "tensor_tensor", xv, xv, ov,
                                      OP.add, inc=(s_vec, 1))
                    x_fin = tape.val(s_vec)
                    xw = [(s_vec, x_fin)]

            tape.enabled = sup < n_sup
            for tt in range(NTT):
                tape.emit("gpsimd", "dma_start",
                          out=out[sup * T_SUP + tt * 128:
                                  sup * T_SUP + (tt + 1) * 128, :],
                          in_=x_sb[:, tt * D:(tt + 1) * D],
                          waits=[(s_vec, x_fin)] if tt == 0 else (),
                          inc=(s_st, 16))
            store_recs.append(tape.val(s_st))

        tape.enabled = True
        final_gps = tape.val(s_st)

        with nc.Block() as block:
            def mk(eng_name):
                def f(eng):
                    for waits, method, args, kwargs, inc in tape.ops[eng_name]:
                        for (sem, v) in waits:
                            eng.wait_ge(sem, v)
                        inst = getattr(eng, method)(*args, **kwargs)
                        if inc is not None:
                            inst.then_inc(inc[0], inc[1])
                    if eng_name == "gpsimd":
                        eng.wait_ge(s_st, final_gps)
                f.__name__ = "_"
                return f

            block.sync(mk("sync"))
            block.tensor(mk("tensor"))
            block.vector(mk("vector"))
            block.scalar(mk("scalar"))
            block.gpsimd(mk("gpsimd"))

    return nc


_EYE_B = np.eye(128).astype(ml_dtypes.bfloat16)
_EYE_F = np.eye(128).astype(np.float32)


def kernel(L0_attn_x, L0_attn_Wq, L0_attn_bq, L0_attn_Wk, L0_attn_bk,
           L0_attn_Wv, L0_attn_bv, L0_ffn_qw_gate, L0_ffn_qw_up,
           L0_ffn_qw_down, L0_ffn_s_gate, L0_ffn_z_gate, L0_ffn_s_up,
           L0_ffn_z_up, L0_ffn_s_down, L0_ffn_z_down):
    x = np.ascontiguousarray(np.asarray(L0_attn_x).reshape(8192, D))
    nc = build_nc()
    shared = {
        "qwg": np.ascontiguousarray(L0_ffn_qw_gate),
        "qwu": np.ascontiguousarray(L0_ffn_qw_up),
        "qwd": np.ascontiguousarray(L0_ffn_qw_down),
        "sg": np.ascontiguousarray(L0_ffn_s_gate),
        "zg": np.ascontiguousarray(L0_ffn_z_gate),
        "su": np.ascontiguousarray(L0_ffn_s_up),
        "zu": np.ascontiguousarray(L0_ffn_z_up),
        "sd": np.ascontiguousarray(L0_ffn_s_down),
        "zd": np.ascontiguousarray(L0_ffn_z_down),
        "wq": np.ascontiguousarray(np.asarray(L0_attn_Wq).reshape(D, 128 * 32)),
        "bq": np.ascontiguousarray(L0_attn_bq),
        "wk": np.ascontiguousarray(np.asarray(L0_attn_Wk).reshape(D, 128)),
        "bk": np.ascontiguousarray(L0_attn_bk),
        "wv": np.ascontiguousarray(np.asarray(L0_attn_Wv).reshape(D, 128)),
        "bv": np.ascontiguousarray(L0_attn_bv),
        "eye_b": _EYE_B,
        "eye_f": _EYE_F,
    }
    in_maps = []
    for c in range(NCORES):
        m = dict(shared)
        m["x"] = np.ascontiguousarray(x[c * TOK:(c + 1) * TOK])
        in_maps.append(m)
    res = run_bass_kernel_spmd(nc, in_maps, core_ids=list(range(NCORES)))
    outs = [np.asarray(r["out"]) for r in res.results]
    return np.concatenate(outs, axis=0).reshape(4, 2048, D).astype(np.float32)


if __name__ == "__main__":
    build_nc()
    print("built ok")
